# version 04
# baseline (speedup 1.0000x reference)
import contextlib
import os
import numpy as np
import concourse.bass as bass
import concourse.mybir as mybir
from concourse.bass_utils import run_bass_kernel_spmd
from concourse.aot_env import allow_ici_all_to_all

F32 = mybir.dt.float32
BF16 = mybir.dt.bfloat16
AF = mybir.ActivationFunctionType
OP = mybir.AluOpType
AX = mybir.AxisListType

NCORES = 8
D = 1024
CH = 8
SEQ = 16384
SW = 512
HALO = 4
SWH = SW + HALO
NST = 4
ED = 1024
NS = 16
RK = 64
DFF = 4096
DI = 2816
CKT = 4096
SC = 512
NSC = CKT // SC
NSL = 16
EPS = 1e-5

VOFF = {n: i * CH for i, n in enumerate(
    ["ln1_g", "ln1_b", "ln2_g", "ln2_b", "rms_w", "conv_b", "dtb", "Dp",
     "fb2", "cvec"])}
NV = len(VOFF)

GRID_T = [(0.25, 0.5), (0.75, 0.5), (1.25, 0.5), (1.75, 0.5),
          (2.5, 1.0), (3.5, 1.0), (5.0, 2.0)]

WQ_GU = 6 * 1024 * 128
WQ_DN = 8 * 384 * 128
WQ_TOT = WQ_GU + WQ_DN


class Buf:
    __slots__ = ("name", "w", "readers")

    def __init__(self, name):
        self.name = name
        self.w = None
        self.readers = []


class Prog:
    def __init__(self, inject=frozenset()):
        self.ops = []
        self.phase = ""
        self.inject = inject
        self.n_real = 0
        self.dummy_fn = None
        self._in_dummy = False

    def E(self, engine, fn, r=(), w=(), kind="c", deps=(), tag="",
          rr=(), bar=()):
        tag = tag or self.phase
        idx = len(self.ops)
        if self._in_dummy:
            my_real = -1
        else:
            my_real = self.n_real
            self.n_real += 1
        dd = set(deps) | set(bar)
        raw = set(deps)
        for b in r:
            if b.w is not None:
                dd.add(b.w)
                raw.add(b.w)
        for b in rr:
            if b.w is not None:
                dd.add(b.w)
        for b in w:
            if b.w is not None:
                dd.add(b.w)
            dd.update(b.readers)
        dd.discard(idx)
        raw.discard(idx)
        self.ops.append(dict(engine=engine, fn=fn, deps=sorted(dd), kind=kind,
                             need=False, tag=tag, bdeps=set(), raw=raw,
                             real=my_real,
                             wn=frozenset(b.name for b in w)))
        for b in r:
            b.readers.append(idx)
        for b in rr:
            b.readers.append(idx)
        for b in w:
            b.w = idx
            b.readers = []
        if my_real >= 0 and my_real in self.inject and self.dummy_fn:
            self._in_dummy = True
            self.dummy_fn()
            self._in_dummy = False
        return idx

    def unprotected_producers(self):
        t_pos = {i: p for p, i in enumerate(
            i for i, op in enumerate(self.ops)
            if op["engine"] == "tensor" and op["kind"] == "c")}
        rdep = {}
        for i, op in enumerate(self.ops):
            for d in list(op["deps"]) + list(op["bdeps"]):
                rdep.setdefault(d, []).append(i)
        bad = set()
        for xi, op in enumerate(self.ops):
            if op["engine"] not in ("scalar", "vector"):
                continue
            waited = set(op["deps"]) | op["bdeps"]
            for d in sorted(op["raw"]):
                dop = self.ops[d]
                if dop["engine"] != "tensor" or dop["kind"] != "c":
                    continue
                p = t_pos[d]
                if not any((x in t_pos and t_pos[x] > p)
                           or (x in rdep.get(d, ())
                               and self.ops[x]["engine"] != op["engine"])
                           for x in waited):
                    if "p_tr" not in dop["wn"]:
                        bad.add(dop["real"])
        bad.discard(-1)
        return bad

    def _cone_max(self, extra):
        ENG = {"sync": 0, "scalar": 1, "tensor": 2, "vector": 3, "gpsimd": 4}
        n = len(self.ops)
        eid = [ENG[op["engine"]] for op in self.ops]
        g = [None] * n
        for _ in range(30):
            changed = False
            prev_idx = [-1] * 5
            for i, op in enumerate(self.ops):
                e = eid[i]
                best = [-1] * 5
                p = prev_idx[e]
                if p >= 0:
                    gp = g[p]
                    for k in range(5):
                        if gp[k] > best[k]:
                            best[k] = gp[k]
                for d in list(op["deps"]) + list(op["bdeps"]):
                    gd = g[d]
                    if gd is None:
                        continue
                    for k in range(5):
                        if gd[k] > best[k]:
                            best[k] = gd[k]
                for d in extra.get(i, ()):
                    gd = g[d]
                    if gd is None:
                        continue
                    for k in range(5):
                        if gd[k] > best[k]:
                            best[k] = gd[k]
                if i > best[e]:
                    best[e] = i
                if g[i] != best:
                    g[i] = best
                    changed = True
                prev_idx[e] = i
            if not changed:
                break
        return g, eid

    def finalize(self):
        t_idx = [i for i, op in enumerate(self.ops)
                 if op["engine"] == "tensor" and op["kind"] == "c"]
        t_pos = {i: p for p, i in enumerate(t_idx)}
        n_hard = 0
        n_skip = 0
        for _round in range(6):
            rdep = {}
            for i, op in enumerate(self.ops):
                for d in list(op["deps"]) + list(op["bdeps"]):
                    rdep.setdefault(d, []).append(i)
            g, eid = self._cone_max({})
            extra = {}
            n_skip = 0
            for xi, op in enumerate(self.ops):
                if op["engine"] not in ("scalar", "vector"):
                    continue
                waited = set(op["deps"]) | op["bdeps"]
                for d in sorted(op["raw"]):
                    dop = self.ops[d]
                    if dop["engine"] != "tensor" or dop["kind"] != "c":
                        continue
                    p = t_pos[d]
                    if any((x in t_pos and t_pos[x] > p)
                           or (x in rdep.get(d, ())
                               and self.ops[x]["engine"] != op["engine"])
                           for x in waited):
                        continue
                    cands = set(t_idx[p + 1:min(p + 9, len(t_idx))])
                    for c in rdep.get(d, ()):
                        if self.ops[c]["engine"] != op["engine"]:
                            cands.add(c)
                    found = None
                    for cand in sorted(cands):
                        if cand == xi or cand in waited:
                            continue
                        if g[cand][eid[xi]] < xi:
                            found = cand
                            break
                    if found is None:
                        n_skip += 1
                    else:
                        extra.setdefault(xi, set()).add(found)
            if not extra:
                break
            for _ in range(10):
                g, eid = self._cone_max(extra)
                dropped = 0
                for xi in list(extra):
                    bad = {N for N in extra[xi] if g[N][eid[xi]] >= xi}
                    if bad:
                        dropped += len(bad)
                        extra[xi] -= bad
                        if not extra[xi]:
                            del extra[xi]
                if not dropped or not extra:
                    break
            for xi, ns in extra.items():
                self.ops[xi]["bdeps"] |= ns
                n_hard += len(ns)
        self._hard_stats = (n_hard, n_skip)
        for op in self.ops:
            if op["kind"] in ("dma", "coll"):
                op["need"] = True
        for op in self.ops:
            for d in list(op["deps"]) + list(op["bdeps"]):
                dop = self.ops[d]
                if (dop["engine"] != op["engine"]
                        or dop["kind"] in ("dma", "coll")
                        or op["kind"] in ("dma", "coll")):
                    dop["need"] = True
        cnt = {}
        for i, op in enumerate(self.ops):
            if not op["need"]:
                op["semkey"] = None
                continue
            if op["kind"] == "coll":
                op["semkey"], op["semval"], op["amt"] = "cc%d" % i, 1, 1
            elif op["kind"] == "dma":
                key = "q_" + op["engine"]
                cnt[key] = cnt.get(key, 0) + 16
                op["semkey"], op["semval"], op["amt"] = key, cnt[key], 16
            else:
                key = "e_" + op["engine"]
                cnt[key] = cnt.get(key, 0) + 1
                op["semkey"], op["semval"], op["amt"] = key, cnt[key], 1

    def semkeys(self):
        return sorted({op["semkey"] for op in self.ops if op["need"]})

    def run_engine(self, engine, eng, sems):
        watermark = {}
        for i, op in enumerate(self.ops):
            if op["engine"] != engine:
                continue
            for d in sorted(set(op["deps"]) | op["bdeps"]):
                dop = self.ops[d]
                if (dop["engine"] == engine and dop["kind"] == "c"
                        and op["kind"] == "c"):
                    continue
                key, val = dop["semkey"], dop["semval"]
                if watermark.get(key, -1) >= val:
                    continue
                eng.wait_ge(sems[key], val)
                watermark[key] = val
            inst = op["fn"](eng)
            if op["need"]:
                inst.then_inc(sems[op["semkey"]], op["amt"])


def build_nc():
    nc1, pr1 = _build_nc(frozenset(), skip_block=True)
    bad = pr1.unprotected_producers()
    nc2, pr2 = _build_nc(frozenset(bad), skip_block=False)
    rem = pr2.unprotected_producers()
    if os.environ.get("MK_DBG"):
        print("drain dummies injected:", len(bad), "remaining unprotected:",
              len(rem))
    return nc2


def _build_nc(inject, skip_block):
    nc = bass.Bass("TRN2", target_bir_lowering=False, debug=False)
    pr = Prog(inject)
    E = pr.E
    bufs = {}

    def B_(name):
        if name not in bufs:
            bufs[name] = Buf(name)
        return bufs[name]

    def dma(queue, out, in_, r=(), w=(), slow=False):
        return E(queue, lambda e, o=out, i=in_, s=slow: e.dma_start(
            out=o, in_=i, allow_slow_non_contiguous=s), r=r, w=w, kind="dma")

    def coll(kind_s, in_ap, out_ap, r=(), w=()):
        def fn(e, k=kind_s, ia=in_ap, oa=out_ap):
            with allow_ici_all_to_all():
                return e.collective_compute(
                    k, OP.bypass, replica_groups=[list(range(NCORES))],
                    ins=[ia.opt()], outs=[oa.opt()])
        return E("gpsimd", fn, r=r, w=w, kind="coll")

    def dram_in(name, shape, dtype=F32):
        return nc.dram_tensor(name, list(shape), dtype, kind="ExternalInput").ap()

    xT = dram_in("xT", [D, NST * SWH])
    A_re = dram_in("A_re", [128, NS])
    wq_gu = dram_in("wq_gu", [6, 128, 1024])
    wq_dn = dram_in("wq_dn", [8, 128, 384])
    wmod_sl = dram_in("wmod_sl", [D, 768])
    w_inproj = dram_in("w_inproj", [D, 2 * ED])
    w_xproj = dram_in("w_xproj", [D, RK + 2 * NS])
    w_dt = dram_in("w_dt", [RK, ED])
    w_out = dram_in("w_out", [ED, D])
    w_ff1 = dram_in("w_ff1", [D, DFF])
    w_ff2 = dram_in("w_ff2", [DFF, D])
    vecs_in = dram_in("vecs", [128, CH * NV])
    fb1_in = dram_in("fb1", [128, DFF // 128])
    convw_in = dram_in("convw", [128, CH * 4])
    ident_in = dram_in("ident", [128, 128])
    ones1_in = dram_in("ones1_in", [1, 128])
    onesum_in = dram_in("onesum_in", [128, 1])
    idrep_in = dram_in("idrep", [128, NSL * 128])
    idB_in = dram_in("idB", [2 * NS, 128])
    idC_in = dram_in("idC", [2 * NS, 128])
    redrep_in = dram_in("redrep", [128, NSL * 128])
    fconst_in = dram_in("fconst", [128, 4])
    fconst2_in = dram_in("fconst2", [128, 3])

    out_fm = nc.dram_tensor("out_fm", [D, NST * SW], F32, kind="ExternalOutput").ap()

    def dram(name, shape, dtype):
        return nc.dram_tensor(name, list(shape), dtype).ap()

    wb_in = dram("wb_in", [D, 2 * ED], BF16)
    wb_out = dram("wb_out", [ED, D], BF16)
    wb_ff1 = dram("wb_ff1", [D, DFF], BF16)
    wb_ff2 = dram("wb_ff2", [DFF, D], BF16)
    hsum_h = dram("hsum_h", [NST, 128, CH * SW], F32)
    xc_h = dram("xc_h", [NST, 128, CH * SW], BF16)
    zs_h = dram("zs_h", [NST, 128, CH * SW], BF16)
    a2a_in = [dram("a2a_in%d" % s, [NCORES, 288, SW], BF16) for s in range(NST)]
    a2a_out = [dram("a2a_out%d" % s, [NCORES, 288, SW], BF16) for s in range(NST)]
    ya_in = [dram("ya_in%d" % k, [NCORES, 128, SW], BF16) for k in range(NST)]
    ya_out = [dram("ya_out%d" % k, [NCORES, 128, SW], BF16) for k in range(NST)]
    mod_in = dram("mod_in", [768], F32)
    mod_out = dram("mod_out", [NCORES, 768], F32)
    wqag_in = dram("wqag_in", [WQ_TOT], BF16)
    wqag_out = dram("wqag_out", [NCORES, WQ_TOT], BF16)
    wgu_lin = dram("wgu_lin", [D, 2 * DI], BF16)
    wdn_lin = dram("wdn_lin", [22 * 128, D], BF16)

    def S(name, shape, dtype=F32):
        return nc.alloc_sbuf_tensor(name, list(shape), dtype)

    eyeb = S("eyeb", [128, 128], BF16)
    ones1 = S("ones1", [1, 128], F32)
    onesum = S("onesum", [128, 1], F32)
    idrep = S("idrep_s", [128, NSL * 128], BF16)
    idBb = S("idBb", [2 * NS, 128], BF16)
    idCb = S("idCb", [2 * NS, 128], BF16)
    redrepb = S("redrepb", [128, NSL * 128], BF16)
    vcs = S("vcs", [128, CH * NV], F32)
    fb1s = S("fb1s", [128, DFF // 128], F32)
    cws = S("cws", [128, CH * 4], F32)
    Ares = S("Ares", [128, NS], F32)
    Amat = S("Amat", [128, NS], F32)
    Amat2 = S("Amat2", [128, NS], F32)
    modr = S("modr", [48, 128], F32)
    modrb = S("modrb", [48, 128], BF16)
    modt = S("modt", [128, 48], F32)
    modf = S("modf", [128, 6 * CH], F32)
    gtmp = S("gtmp", [128, CH], F32)
    gtmp2 = S("gtmp2", [128, CH], F32)
    silc = S("silc", [128, CH], F32)
    stat = S("stat", [1, 4 * SW], F32)
    state = S("state", [128, NS], F32)
    w_xp = S("w_xp", [128, CH * 96], BF16)
    w_dts = S("w_dts", [RK, ED], BF16)
    MODEV = S("MODEV", [128, 6], F32)
    FCONST = S("FCONST", [128, 4], F32)
    FCONST2 = S("FCONST2", [128, 3], F32)

    XTs = S("XTs", [128, CH * SWH], F32)
    Hs = S("Hs", [128, CH * SWH], F32)
    H1s = S("H1s", [128, CH * SWH], F32)
    HMs = S("HMs", [128, CH * SWH], BF16)
    XIs = S("XIs", [128, CH * SWH], BF16)
    XCs = S("XCs", [128, CH * SW], BF16)
    X2s = XTs
    XMs = Hs
    FMs = H1s.bitcast(BF16)
    Y2s = HMs
    H3T = XIs
    H3Q = XCs

    SQ = S("SQ", [128, SWH], F32)
    SQ2 = S("SQ2", [128, SWH], F32)
    DTB = S("DTB", [96, SW], BF16)
    zroll = S("zroll", [128, SW], BF16)
    dlroll = S("dlroll", [128, SW], BF16)
    uuroll = S("uuroll", [128, SW], BF16)
    winroll = [S("winroll%d" % i, [128, CH * 128], BF16) for i in range(2)]

    CASTB = S("CASTB", [128, 1024], BF16)
    QSCr = S("QSCr", [128, 64], F32)
    QSC = S("QSC", [128, 64], F32)
    QRS = S("QRS", [128, 64], F32)
    QV = S("QV", [128, 512], F32)
    QA = S("QA", [128, 512], F32)
    QACC = S("QACC", [128, 512], BF16)
    QAC2 = S("QAC2", [128, 512], BF16)
    QSG = S("QSG", [128, 512], BF16)
    QSG2 = S("QSG2", [128, 512], BF16)
    QT = S("QT", [128, 1024], BF16)
    WQT = S("WQT", [128, 1024], BF16)
    wqroll = S("wqroll", [128, 1024], BF16)

    dcroll = [S("dcroll%d" % i, [128, SC], BF16) for i in range(2)]
    ucroll = [S("ucroll%d" % i, [128, SC], BF16) for i in range(2)]
    bcroll = [S("bcroll%d" % i, [2 * NS, SC], BF16) for i in range(2)]
    BTt = S("BTt", [128, SC], BF16)
    CTt = S("CTt", [128, SC], BF16)
    DAt = S("DAt", [128, SC], BF16)
    BXt = S("BXt", [128, SC], BF16)
    HSt = S("HSt", [128, SC], BF16)
    HCt = S("HCt", [128, SC], BF16)
    ycroll = S("ycroll", [128, SC], BF16)

    POOL_A = S("POOL_A", [128, 22 * SW], BF16)
    H2s = POOL_A
    HMQT = POOL_A
    POOL_B = S("POOL_B", [128, CH * SW], BF16)
    H3s = POOL_B
    HMT = POOL_B
    YTK = S("YTK", [128, SW], BF16)
    XCr = S("XCr", [128, SW], BF16)
    ZSr = S("ZSr", [128, SW], BF16)
    HSr = S("HSr", [128, SW], F32)
    XINr = S("XINr", [128, SW], F32)
    GSs = S("GSs", [128, SC], BF16)
    OUTr = S("OUTr", [128, SW], F32)
    woutroll = [S("woutroll0", [128, CH * 128], BF16)] * 2
    w1roll = [S("w1roll0", [128, CH * 128], BF16)] * 2
    w2roll = S("w2roll", [128, 16 * 128], BF16)
    wguroll = [S("wguroll%d" % i, [128, CH * SC], BF16) for i in range(2)]
    wdnroll = [S("wdnroll0", [128, 22 * 128], BF16)] * 2
    WMODr = wguroll[0].bitcast(F32)

    def PS(name, shape, dtype=F32):
        return nc.alloc_psum_tensor(name, list(shape), dtype)

    p_statA = PS("p_statA", [1, SW], F32)
    p_statB = PS("p_statB", [1, SW], F32)
    p_repA = PS("p_repA", [128, SW], F32)
    p_repB = PS("p_repB", [128, SW], F32)
    p_mainA = PS("p_mainA", [128, SW], F32)
    p_mainB = PS("p_mainB", [128, SW], F32)
    p_y = PS("p_y", [128, SC], F32)
    p_tr = PS("p_tr", [128, 1024], BF16)

    p_main = [p_mainA, p_mainB]
    bMain = [B_("p_mainA"), B_("p_mainB")]
    bStatA, bStatB = B_("p_statA"), B_("p_statB")
    bRepA, bRepB = B_("p_repA"), B_("p_repB")
    bStat, bSQ, bSQ2 = B_("stat"), B_("SQ"), B_("SQ2")
    bTR, bYps = B_("p_tr"), B_("p_y")
    bVcs, bModf = B_("vcs"), B_("modf")
    pr.dummy_fn = lambda: E(
        "tensor", lambda e: e.transpose(p_tr[0:32, 0:32], eyeb[0:32, 0:32],
                                        eyeb[0:32, 0:32]),
        r=[B_("eyeb")], w=[bTR], tag="drain_dummy")
    bXT = B_("XT_X2")
    bH = B_("H_XM")
    bH1 = B_("H1_FM")
    bHM = B_("HM_Y2")
    bXI = B_("XI_H3T")
    bXC = B_("XC_H3Q")
    bPA = B_("POOL_A")
    bPB = B_("POOL_B")

    pr.phase = "P0_const"
    bWhbm = B_("weights_hbm_bf16")
    dma("sync", ones1[:], ones1_in, w=[B_("ones1")])
    dma("sync", onesum[:], onesum_in, w=[B_("onesum")])
    dma("sync", vcs[:], vecs_in, w=[bVcs])
    dma("sync", fb1s[:], fb1_in, w=[B_("fb1s")])
    dma("sync", cws[:], convw_in, w=[B_("cws")])
    dma("sync", Ares[:], A_re, w=[B_("Ares")])
    dma("sync", FCONST[:], fconst_in, w=[B_("fconst")])
    dma("sync", FCONST2[:], fconst2_in, w=[B_("fconst")])
    nc.const_aps.register(F32, EPS, FCONST[:, 0:1])
    nc.const_aps.register(F32, 1.0, FCONST[:, 1:2])
    nc.const_aps.register(F32, 1e-12, FCONST[:, 2:3])
    nc.const_aps.register(F32, -1e-12, FCONST[:, 3:4])
    nc.const_aps.register(F32, -1.0, FCONST2[:, 0:1])
    nc.const_aps.register(F32, -0.25, FCONST2[:, 1:2])
    nc.const_aps.register(F32, 0.25, FCONST2[:, 2:3])
    dma("gpsimd", eyeb[:], ident_in, w=[B_("eyeb")])
    dma("gpsimd", idrep[:], idrep_in, w=[B_("idrep")])
    dma("gpsimd", idBb[:], idB_in, w=[B_("idBb")])
    dma("gpsimd", idCb[:], idC_in, w=[B_("idCb")])
    dma("gpsimd", redrepb[:], redrep_in, w=[B_("redrepb")])
    dma("gpsimd", w_xp[:].rearrange("p (k c) -> p k c", k=CH),
        w_xproj.rearrange("(k p) c -> k p c", p=128).transpose([1, 0, 2]),
        w=[B_("w_xp")])
    dma("gpsimd", w_dts[:], w_dt, w=[B_("w_dts")])

    E("scalar", lambda e: e.activation(out=Amat[:], in_=Ares[:], func=AF.Exp),
      r=[B_("Ares")], w=[B_("Amat")])
    E("vector", lambda e: e.tensor_scalar_mul(Amat2[:], Amat[:], -1.0),
      r=[B_("Amat")], w=[B_("Amat2")])

    pr.phase = "P0_cast"
    bCast = B_("CASTB")
    cast_jobs = []
    for (src, dst, nsl, cols) in [(w_inproj, wb_in, 8, 2 * ED),
                                  (w_out, wb_out, 8, D),
                                  (w_ff1, wb_ff1, 8, DFF),
                                  (w_ff2, wb_ff2, 32, D)]:
        for r_i in range(nsl):
            for c0 in range(0, cols, 1024):
                cw = min(1024, cols - c0)
                cast_jobs.append((src, dst, r_i, c0, cw))
    for (src, dst, r_i, c0, cw) in cast_jobs:
        dma("gpsimd", CASTB[:, 0:cw],
            src[r_i * 128:(r_i + 1) * 128, c0:c0 + cw], w=[bCast])
        dma("sync", dst[r_i * 128:(r_i + 1) * 128, c0:c0 + cw],
            CASTB[:, 0:cw], r=[bCast], w=[bWhbm])

    pr.phase = "P0_mod"
    E("scalar", lambda e: e.activation(
        out=silc[:], in_=vcs[:, VOFF["cvec"]:VOFF["cvec"] + CH], func=AF.Silu),
      r=[bVcs], w=[B_("silc")])
    bWGU0 = B_("wguroll0")
    for k in range(CH):
        dma("sync", WMODr[:, 0:768], wmod_sl[k * 128:(k + 1) * 128, :],
            w=[bWGU0])
        def mk_modk(kk):
            def fn(e):
                last = None
                for m in range(6):
                    last = e.matmul(p_mainA[:, m:m + 1],
                                    WMODr[:, m * 128:(m + 1) * 128],
                                    silc[:, kk:kk + 1],
                                    start=(kk == 0 and m == 0),
                                    stop=(kk == CH - 1),
                                    skip_group_check=True)
                return last
            return fn
        E("tensor", mk_modk(k), r=[bWGU0, B_("silc")], w=[bMain[0]])
    dum_mod = E("tensor", lambda e: e.matmul(
        p_statA[0:1, 0:8], onesum[:], FCONST[:, 0:1].broadcast_to([128, 8]),
        start=True, stop=True),
        r=[B_("onesum"), B_("fconst")], w=[bStatA])
    E("scalar", lambda e: e.copy(MODEV[:], p_mainA[:, 0:6]),
      r=[bMain[0]], w=[B_("MODEV")], deps=[dum_mod])
    dma("scalar", mod_in.rearrange("(m p) -> p m", p=128), MODEV[:],
        r=[B_("MODEV")], w=[B_("mod_in")], slow=True)
    coll("AllGather", mod_in, mod_out, r=[B_("mod_in")], w=[B_("mod_out")])
    dma("sync", modr[:], mod_out.rearrange("c (r p) -> (c r) p", p=128),
        r=[B_("mod_out")], w=[B_("modr")])
    E("vector", lambda e: e.tensor_copy(modrb[:], modr[:]),
      r=[B_("modr")], w=[B_("modrb")])
    E("tensor", lambda e: e.transpose(p_tr[:, 0:48], modrb[:], eyeb[0:48, 0:48]),
      r=[B_("modrb"), B_("eyeb")], w=[bTR])
    E("scalar", lambda e: e.copy(modt[:], p_tr[:, 0:48]),
      r=[bTR], w=[B_("modt")])
    bModt = B_("modt")
    E("vector", lambda e: e.tensor_copy(modf[:, 0:CH], modt[:, 0:CH]),
      r=[bModt], w=[bModf])
    E("vector", lambda e: e.tensor_scalar_add(
        modf[:, CH:2 * CH], modt[:, CH:2 * CH], 1.0), r=[bModt], w=[bModf])
    E("vector", lambda e: e.tensor_copy(
        modf[:, 3 * CH:4 * CH], modt[:, 3 * CH:4 * CH]), r=[bModt], w=[bModf])
    E("vector", lambda e: e.tensor_scalar_add(
        modf[:, 4 * CH:5 * CH], modt[:, 4 * CH:5 * CH], 1.0),
      r=[bModt], w=[bModf])
    for g in (2, 5):
        E("vector", lambda e, gg=g: e.tensor_scalar(
            out=gtmp[:], in0=modt[:, gg * CH:(gg + 1) * CH],
            scalar1=0.0, scalar2=2.0, op0=OP.is_ge, op1=OP.mult),
          r=[bModt], w=[B_("gtmp")])
        E("scalar", lambda e: e.activation(
            out=gtmp2[:], in_=gtmp[:], func=AF.Identity, bias=-1.0),
          r=[B_("gtmp"), B_("fconst")], w=[B_("gtmp2")])
        E("scalar", lambda e, gg=g: e.activation(
            out=gtmp[:], in_=modt[:, gg * CH:(gg + 1) * CH], func=AF.Abs),
          r=[bModt], w=[B_("gtmp")])
        E("scalar", lambda e: e.activation(
            out=gtmp[:], in_=gtmp[:], func=AF.Relu, bias=-0.25),
          r=[B_("gtmp"), B_("fconst")], w=[B_("gtmp")])
        E("scalar", lambda e: e.activation(
            out=gtmp[:], in_=gtmp[:], func=AF.Identity, bias=0.25),
          r=[B_("gtmp"), B_("fconst")], w=[B_("gtmp")])
        E("vector", lambda e, gg=g: e.tensor_mul(
            modf[:, gg * CH:(gg + 1) * CH], gtmp2[:], gtmp[:]),
          r=[B_("gtmp2"), B_("gtmp")], w=[bModf])

    pr.phase = "P0_wq"
    bQSC, bQRS, bQV, bQA, bQACC, bQSG = (B_("QSC"), B_("QRS"), B_("QV"),
                                         B_("QA"), B_("QACC"), B_("QSG"))
    bQAC2 = B_("QAC2")
    bQT, bWQT, bWQAGI = B_("QT"), B_("WQT"), B_("wqag_in")
    bWQR = B_("wqroll")

    def emit_quant(src_t, src_off, ncols, src_buf, dst_t, dst_off, dst_buf):
        for p0 in range(0, ncols, 512):
            pw = min(512, ncols - p0)
            blk = pw // 16
            sap = src_t[:, src_off + p0:src_off + p0 + pw]
            dap = dst_t[:, dst_off + p0:dst_off + p0 + pw]
            E("vector", lambda e, sap=sap, blk=blk: e.tensor_reduce(
                out=QSCr[:, 0:blk],
                in_=sap.rearrange("p (nb b) -> p nb b", b=16),
                axis=AX.X, op=OP.max, apply_absolute_value=True),
              r=[src_buf], w=[B_("QSCr")])
            E("scalar", lambda e, blk=blk: e.activation(
                out=QSC[:, 0:blk], in_=QSCr[:, 0:blk], func=AF.Relu,
                scale=1.0 / 6.0, bias=-1e-12),
              r=[B_("QSCr"), B_("fconst")], w=[bQSC])
            E("scalar", lambda e, blk=blk: e.activation(
                out=QSC[:, 0:blk], in_=QSC[:, 0:blk], func=AF.Identity,
                bias=1e-12),
              r=[bQSC, B_("fconst")], w=[bQSC])
            E("scalar", lambda e, blk=blk: e.activation(
                out=QRS[:, 0:blk], in_=QSC[:, 0:blk], func=AF.Ln),
              r=[bQSC], w=[bQRS])
            E("scalar", lambda e, blk=blk: e.activation(
                out=QRS[:, 0:blk], in_=QRS[:, 0:blk], func=AF.Exp, scale=-1.0),
              r=[bQRS], w=[bQRS])
            E("vector", lambda e, sap=sap, blk=blk, pw=pw: e.tensor_tensor(
                out=QV[:, 0:pw].rearrange("p (nb b) -> p nb b", b=16),
                in0=sap.rearrange("p (nb b) -> p nb b", b=16),
                in1=QRS[:, 0:blk, None].broadcast_to([128, blk, 16]),
                op=OP.mult), r=[src_buf, bQRS], w=[bQV])
            E("scalar", lambda e, pw=pw: e.activation(
                out=QA[:, 0:pw], in_=QV[:, 0:pw], func=AF.Abs),
              r=[bQV], w=[bQA])
            accs = [QACC, QAC2]
            baccs = [bQACC, bQAC2]
            cur = 0
            for ti, (thr, dlt) in enumerate(GRID_T):
                if ti == 0:
                    E("vector", lambda e, t=thr, dl=dlt, pw=pw: e.tensor_scalar(
                        out=QACC[:, 0:pw], in0=QA[:, 0:pw], scalar1=t,
                        scalar2=dl, op0=OP.is_gt, op1=OP.mult),
                      r=[bQA], w=[bQACC])
                else:
                    E("vector", lambda e, t=thr, dl=dlt, pw=pw: e.tensor_scalar(
                        out=QSG[:, 0:pw], in0=QA[:, 0:pw], scalar1=t,
                        scalar2=dl, op0=OP.is_gt, op1=OP.mult),
                      r=[bQA], w=[bQSG])
                    nxt = 1 - cur
                    E("vector", lambda e, pw=pw, a=accs[cur], b=accs[nxt]:
                      e.tensor_add(b[:, 0:pw], QSG[:, 0:pw], a[:, 0:pw]),
                      r=[bQSG, baccs[cur]], w=[baccs[nxt]])
                    cur = nxt
            E("vector", lambda e, pw=pw: e.tensor_scalar(
                out=QSG[:, 0:pw], in0=QV[:, 0:pw], scalar1=0.0, scalar2=-2.0,
                op0=OP.is_lt, op1=OP.mult), r=[bQV], w=[bQSG])
            E("vector", lambda e, pw=pw: e.tensor_scalar_add(
                QSG2[:, 0:pw], QSG[:, 0:pw], 1.0), r=[bQSG], w=[B_("QSG2")])
            nxt = 1 - cur
            E("vector", lambda e, pw=pw, a=accs[cur], b=accs[nxt]: e.tensor_mul(
                b[:, 0:pw], QSG2[:, 0:pw], a[:, 0:pw]),
              r=[B_("QSG2"), baccs[cur]], w=[baccs[nxt]])
            cur = nxt
            E("vector", lambda e, dap=dap, blk=blk, pw=pw, a=accs[cur]:
              e.tensor_tensor(
                out=dap.rearrange("p (nb b) -> p nb b", b=16),
                in0=a[:, 0:pw].rearrange("p (nb b) -> p nb b", b=16),
                in1=QSC[:, 0:blk, None].broadcast_to([128, blk, 16]),
                op=OP.mult), r=[baccs[cur], bQSC], w=[dst_buf])

    def tr_seq(items):
        toks = []
        for (s, sb, sl, d, db) in items:
            toks.append(E("tensor", lambda e, s=s, sl=sl: e.transpose(
                p_tr[:, sl * 128:(sl + 1) * 128], s, eyeb[:]),
                r=[sb, B_("eyeb")], w=[bTR]))
        n = len(items)
        for i, (s, sb, sl, d, db) in enumerate(items):
            if i < n - 1:
                E("scalar", lambda e, d=d, sl=sl: e.copy(
                    d, p_tr[:, sl * 128:(sl + 1) * 128]),
                  rr=[bTR], w=[db], bar=[toks[i + 1]])
            else:
                dum = E("tensor", lambda e, sl=items[0][2]: e.transpose(
                    p_tr[0:32, sl * 128:sl * 128 + 32], eyeb[0:32, 0:32],
                    eyeb[0:32, 0:32]),
                        r=[B_("eyeb")], w=[bTR])
                E("scalar", lambda e, d=d, sl=sl: e.copy(
                    d, p_tr[:, sl * 128:(sl + 1) * 128]),
                  rr=[bTR], w=[db], bar=[dum])

    def tr_block(src_ap, src_buf, slot, dst_ap, dst_buf):
        tr_seq([(src_ap, src_buf, slot, dst_ap, dst_buf)])

    for t in range(6):
        dma("gpsimd", wqroll[:], wq_gu[t], w=[bWQR])
        emit_quant(wqroll, 0, 1024, bWQR, QT, 0, bQT)
        tr_seq([(QT[:, kb * 128:(kb + 1) * 128], bQT, kb,
                 WQT[:, kb * 128:(kb + 1) * 128], bWQT) for kb in range(8)])
        dma("scalar",
            wqag_in[t * 1024 * 128:(t + 1) * 1024 * 128]
            .rearrange("(k p c) -> p k c", p=128, c=128),
            WQT[:].rearrange("p (k c) -> p k c", k=8),
            r=[bWQT], w=[bWQAGI])
    for t in range(8):
        dma("gpsimd", wqroll[:, 0:384], wq_dn[t], w=[bWQR])
        emit_quant(wqroll, 0, 384, bWQR, QT, 0, bQT)
        tr_seq([(QT[:, kb * 128:(kb + 1) * 128], bQT, kb,
                 WQT[:, kb * 128:(kb + 1) * 128], bWQT) for kb in range(3)])
        dma("scalar",
            wqag_in[WQ_GU + t * 384 * 128:WQ_GU + (t + 1) * 384 * 128]
            .rearrange("(k p c) -> p k c", p=128, c=128),
            WQT[:, 0:384].rearrange("p (k c) -> p k c", k=3),
            r=[bWQT], w=[bWQAGI])
    coll("AllGather", wqag_in, wqag_out, r=[bWQAGI], w=[B_("wqag_out")])
    bGU, bDN = B_("wgu_lin"), B_("wdn_lin")
    wq_flat = wqag_out.rearrange("r f -> (r f)")
    for tg in range(44):
        r_, s_ = tg // 6, tg % 6
        off = r_ * WQ_TOT + s_ * 1024 * 128
        dma("gpsimd", wgu_lin[:, tg * 128:(tg + 1) * 128],
            wq_flat[off:off + 1024 * 128].rearrange("(p c) -> p c", c=128),
            r=[B_("wqag_out")], w=[bGU])
    for r_ in range(8):
        for rt in range(8):
            off = r_ * WQ_TOT + WQ_GU + rt * 384 * 128
            dma("gpsimd",
                wdn_lin[r_ * 352:(r_ + 1) * 352, rt * 128:(rt + 1) * 128],
                wq_flat[off:off + 352 * 128].rearrange("(p c) -> p c", c=128),
                r=[B_("wqag_out")], w=[bDN])

    def mk_mm_kchain(psum_ap, stat_tensor, stat_base, stat_stride, mov_tensor,
                     mov_base, mov_stride, nk, W, m_w=128):
        def fn(e, psum_ap=psum_ap, stat_tensor=stat_tensor,
               stat_base=stat_base, stat_stride=stat_stride,
               mov_tensor=mov_tensor, mov_base=mov_base,
               mov_stride=mov_stride, nk=nk, W=W, m_w=m_w):
            last = None
            for k in range(nk):
                last = e.matmul(
                    psum_ap,
                    stat_tensor[:, stat_base + k * stat_stride:
                                stat_base + k * stat_stride + m_w],
                    mov_tensor[:, mov_base + k * mov_stride:
                               mov_base + k * mov_stride + W],
                    start=(k == 0), stop=(k == nk - 1))
            return last
        return fn

    def norm_stats(src, src_buf, cols, off, W, do_mean, rep=None):
        W = SW
        pA, bA, pB, bB = rep if rep else (p_repA, bRepA, p_repB, bRepB)
        def mk_sum(e, src=src, cols=cols, off=off, W=W):
            last = None
            for k in range(CH):
                last = e.matmul(p_statA[0:1, 0:W], onesum[:],
                                src[:, k * cols + off:k * cols + off + W],
                                start=(k == 0), stop=(k == CH - 1))
            return last
        if do_mean:
            E("tensor", mk_sum, r=[src_buf, B_("onesum")], w=[bStatA])
        for k in range(CH):
            E("scalar", lambda e, kk=k, src=src, cols=cols, off=off, W=W:
              e.activation(out=SQ2[:, 0:W],
                           in_=src[:, kk * cols + off:kk * cols + off + W],
                           func=AF.Square),
              r=[src_buf], w=[bSQ2])
            E("tensor", lambda e, kk=k, W=W: e.matmul(
                p_statB[0:1, 0:W], onesum[:], SQ2[:, 0:W],
                start=(kk == 0), stop=(kk == CH - 1), skip_group_check=True),
              r=[bSQ2, B_("onesum")], w=[bStatB])
        MU, VAR, SCR, RST = 0, SW, 2 * SW, 3 * SW
        if do_mean:
            E("vector", lambda e, W=W: e.tensor_scalar_mul(
                stat[0:1, MU:MU + W], p_statA[0:1, 0:W], 1.0 / D),
              r=[bStatA], w=[bStat])
            E("vector", lambda e, W=W: e.tensor_scalar_mul(
                stat[0:1, VAR:VAR + W], p_statB[0:1, 0:W], 1.0 / D),
              r=[bStatB], w=[bStat])
            E("vector", lambda e, W=W: e.tensor_mul(
                stat[0:1, SCR:SCR + W], stat[0:1, MU:MU + W],
                stat[0:1, MU:MU + W]), r=[bStat], w=[bStat])
            E("vector", lambda e, W=W: e.tensor_sub(
                stat[0:1, RST:RST + W], stat[0:1, VAR:VAR + W],
                stat[0:1, SCR:SCR + W]), r=[bStat], w=[bStat])
        else:
            E("vector", lambda e, W=W: e.tensor_scalar_mul(
                stat[0:1, RST:RST + W], p_statB[0:1, 0:W], 1.0 / D),
              r=[bStatB], w=[bStat])
        E("scalar", lambda e, W=W: e.activation(
            out=stat[0:1, RST:RST + W], in_=stat[0:1, RST:RST + W],
            func=AF.Ln, bias=EPS), r=[bStat, B_("fconst")], w=[bStat])
        E("scalar", lambda e, W=W: e.activation(
            out=stat[0:1, RST:RST + W], in_=stat[0:1, RST:RST + W],
            func=AF.Exp, scale=-0.5), r=[bStat], w=[bStat])
        E("tensor", lambda e, W=W, pA=pA: e.matmul(
            pA[:, 0:W], ones1[:], stat[0:1, RST:RST + W],
            start=True, stop=True), r=[bStat, B_("ones1")], w=[bA])
        if do_mean:
            E("vector", lambda e, W=W: e.tensor_mul(
                stat[0:1, SCR:SCR + W], stat[0:1, MU:MU + W],
                stat[0:1, RST:RST + W]), r=[bStat], w=[bStat])
            E("tensor", lambda e, W=W, pB=pB: e.matmul(
                pB[:, 0:W], ones1[:], stat[0:1, SCR:SCR + W],
                start=True, stop=True), r=[bStat, B_("ones1")], w=[bB])

    def ln_apply_chunk(e_args):
        pass

    def ln_apply(srcT, src_buf, cols, off, W, dstT, dst_buf, dcols, doff,
                 gv, bv, sub_mean=True, extra_rbufs=(), rep=None):
        pA, bA, pB, bB = rep if rep else (p_repA, bRepA, p_repB, bRepB)
        for k in range(CH):
            E("vector", lambda e, kk=k, srcT=srcT, cols=cols, off=off, W=W,
              pA=pA: e.tensor_tensor(
                out=SQ2[:, 0:W],
                in0=srcT[:, kk * cols + off:kk * cols + off + W],
                in1=pA[:, 0:W], op=OP.mult),
              r=[src_buf, bA], w=[bSQ2], tag="lnap_mult_k%d" % k)
            fin, finb = (SQ2, bSQ2)
            if sub_mean:
                E("vector", lambda e, W=W, pB=pB: e.tensor_sub(
                    SQ[:, 0:W], SQ2[:, 0:W], pB[:, 0:W]),
                  r=[bB, bSQ2], w=[bSQ], tag="lnap_sub_k%d" % k)
                fin, finb = (SQ, bSQ)
            gt, gc = gv
            if bv is not None:
                bt, bc = bv
                E("vector", lambda e, kk=k, W=W, gt=gt, gc=gc, bt=bt, bc=bc,
                  dstT=dstT, dcols=dcols, doff=doff, fin=fin: e.tensor_scalar(
                    out=dstT[:, kk * dcols + doff:kk * dcols + doff + W],
                    in0=fin[:, 0:W],
                    scalar1=gt[:, gc + kk:gc + kk + 1],
                    scalar2=bt[:, bc + kk:bc + kk + 1],
                    op0=OP.mult, op1=OP.add),
                  r=[finb] + list(extra_rbufs), w=[dst_buf], tag="lnap_ts_k%d" % k)
            else:
                E("vector", lambda e, kk=k, W=W, gt=gt, gc=gc, dstT=dstT,
                  dcols=dcols, doff=doff, fin=fin: e.tensor_scalar(
                    out=dstT[:, kk * dcols + doff:kk * dcols + doff + W],
                    in0=fin[:, 0:W],
                    scalar1=gt[:, gc + kk:gc + kk + 1], scalar2=None,
                    op0=OP.mult),
                  r=[finb] + list(extra_rbufs), w=[dst_buf])

    bDTB = B_("DTB")
    bZR, bDLR, bUUR = B_("zroll"), B_("dlroll"), B_("uuroll")
    bWinR = [B_("winroll0"), B_("winroll1")]

    for s in range(NST):
        pr.phase = "P1_s%d" % s
        dma("sync", XTs[:].rearrange("p (k c) -> p k c", k=CH),
            xT.rearrange("(k p) c -> k p c", p=128)
            .transpose([1, 0, 2])[:, :, s * SWH:(s + 1) * SWH],
            w=[bXT])
        for (off, W, is_halo) in [(0, 64, True), (HALO, SW, False)]:
            norm_stats(XTs, bXT, SWH, off, W, True)
            ln_apply(XTs, bXT, SWH, off, W, Hs, bH, SWH, off,
                     (modf, CH), (modf, 0), extra_rbufs=[bModf])
            norm_stats(Hs, bH, SWH, off, W, True)
            ln_apply(Hs, bH, SWH, off, W, H1s, bH1, SWH, off,
                     (vcs, VOFF["ln1_g"]), (vcs, VOFF["ln1_b"]),
                     extra_rbufs=[bVcs])
            if not is_halo:
                for k in range(CH):
                    E("vector", lambda e, kk=k: e.tensor_add(
                        Hs[:, kk * SWH + HALO:kk * SWH + HALO + SW],
                        H1s[:, kk * SWH + HALO:kk * SWH + HALO + SW],
                        Hs[:, kk * SWH + HALO:kk * SWH + HALO + SW]),
                      r=[bH, bH1], w=[bH])
                dma("scalar", hsum_h[s].rearrange("p (k c) -> p k c", k=CH),
                    Hs[:].rearrange("p (k c) -> p k c", k=CH)[:, :, HALO:SWH],
                    r=[bH], w=[B_("hsum_h%d" % s)])
            norm_stats(H1s, bH1, SWH, off, W, False)
            ln_apply(H1s, bH1, SWH, off, W, HMs, bHM, SWH, off,
                     (vcs, VOFF["rms_w"]), None, sub_mean=False,
                     extra_rbufs=[bVcs])
        for mc in range(2 * CH):
            wt, wb = winroll[mc % 2], bWinR[mc % 2]
            dma("sync", wt[:].rearrange("p (k c) -> p k c", k=CH),
                wb_in.rearrange("(k p) c -> k p c", p=128)
                .transpose([1, 0, 2])[:, :, mc * 128:(mc + 1) * 128],
                r=[bWhbm], w=[wb])
            pb, pt = bMain[mc % 2], p_main[mc % 2]
            if mc < CH:
                E("tensor", mk_mm_kchain(pt[:, 0:HALO], wt, 0, 128,
                                         HMs, 0, SWH, CH, HALO),
                  r=[wb, bHM], w=[pb])
                E("scalar", lambda e, mm=mc, pt=pt: e.copy(
                    XIs[:, mm * SWH:mm * SWH + HALO], pt[:, 0:HALO]),
                  r=[pb], w=[bXI])
            E("tensor", mk_mm_kchain(pt[:, 0:SW], wt, 0, 128,
                                     HMs, HALO, SWH, CH, SW),
              r=[wb, bHM], w=[pb])
            if mc < CH:
                E("scalar", lambda e, mm=mc, pt=pt: e.copy(
                    XIs[:, mm * SWH + HALO:(mm + 1) * SWH], pt[:, 0:SW]),
                  r=[pb], w=[bXI])
            else:
                E("scalar", lambda e, pt=pt: e.activation(
                    out=zroll[:], in_=pt[:, 0:SW], func=AF.Silu),
                  r=[pb], w=[bZR])
                dma("scalar", zs_h[s][:, (mc - CH) * SW:(mc - CH + 1) * SW],
                    zroll[:], r=[bZR], w=[B_("zs_h%d" % s)])
        CO = HALO - 3
        for k in range(CH):
            E("vector", lambda e, kk=k: e.tensor_scalar(
                out=SQ[:, 0:SW], in0=XIs[:, kk * SWH + CO:kk * SWH + CO + SW],
                scalar1=cws[:, kk * 4:kk * 4 + 1], scalar2=None, op0=OP.mult),
              r=[bXI, B_("cws")], w=[bSQ])
            for j in range(1, 4):
                E("vector", lambda e, kk=k, jj=j: e.scalar_tensor_tensor(
                    out=SQ[:, 0:SW],
                    in0=XIs[:, kk * SWH + CO + jj:kk * SWH + CO + jj + SW],
                    scalar=cws[:, kk * 4 + jj:kk * 4 + jj + 1],
                    in1=SQ[:, 0:SW], op0=OP.mult, op1=OP.add),
                  r=[bXI, B_("cws"), bSQ], w=[bSQ])
            E("scalar", lambda e, kk=k: e.activation(
                out=XCs[:, kk * SW:(kk + 1) * SW], in_=SQ[:, 0:SW],
                func=AF.Silu,
                bias=vcs[:, VOFF["conv_b"] + kk:VOFF["conv_b"] + kk + 1],
                scale=1.0),
              r=[bSQ, bVcs], w=[bXC])
        dma("scalar", xc_h[s], XCs[:], r=[bXC], w=[B_("xc_h%d" % s)])
        E("tensor", mk_mm_kchain(p_mainA[0:96, 0:SW], w_xp, 0, 96,
                                 XCs, 0, SW, CH, SW, m_w=96),
          r=[B_("w_xp"), bXC], w=[bMain[0]])
        E("scalar", lambda e: e.copy(DTB[0:96, 0:SW], p_mainA[0:96, 0:SW]),
          r=[bMain[0]], w=[bDTB])
        bA2I = B_("a2a_in%d" % s)
        for mc in range(CH):
            E("tensor", lambda e, mm=mc: e.matmul(
                p_mainB[:, 0:SW], w_dts[:, mm * 128:(mm + 1) * 128],
                DTB[0:64, 0:SW], start=True, stop=True),
              r=[B_("w_dts"), bDTB], w=[bMain[1]])
            E("scalar", lambda e, mm=mc: e.activation(
                out=SQ[:, 0:SW], in_=p_mainB[:, 0:SW], func=AF.Abs,
                bias=vcs[:, VOFF["dtb"] + mm:VOFF["dtb"] + mm + 1], scale=1.0),
              r=[bMain[1], bVcs], w=[bSQ])
            E("scalar", lambda e: e.activation(
                out=SQ[:, 0:SW], in_=SQ[:, 0:SW], func=AF.Exp, scale=-1.0),
              r=[bSQ], w=[bSQ])
            E("scalar", lambda e: e.activation(
                out=SQ[:, 0:SW], in_=SQ[:, 0:SW], func=AF.Ln, bias=1.0),
              r=[bSQ, B_("fconst")], w=[bSQ])
            E("scalar", lambda e, mm=mc: e.activation(
                out=SQ2[:, 0:SW], in_=p_mainB[:, 0:SW], func=AF.Relu,
                bias=vcs[:, VOFF["dtb"] + mm:VOFF["dtb"] + mm + 1], scale=1.0),
              r=[bMain[1], bVcs], w=[bSQ2])
            E("vector", lambda e: e.tensor_tensor(
                out=dlroll[:], in0=SQ2[:, 0:SW], in1=SQ[:, 0:SW], op=OP.add),
              r=[bSQ2, bSQ], w=[bDLR])
            E("vector", lambda e, mm=mc: e.tensor_tensor(
                out=uuroll[:], in0=dlroll[:],
                in1=XCs[:, mm * SW:(mm + 1) * SW], op=OP.mult),
              r=[bDLR, bXC], w=[bUUR])
            dma("scalar", a2a_in[s][mc, 0:128, :], dlroll[:],
                r=[bDLR], w=[bA2I])
            dma("scalar", a2a_in[s][mc, 128:256, :], uuroll[:],
                r=[bUUR], w=[bA2I])
        for j in range(NCORES):
            dma("scalar", a2a_in[s][j, 256:288, :], DTB[64:96, 0:SW],
                r=[bDTB], w=[bA2I])
        coll("AllToAll", a2a_in[s], a2a_out[s], r=[bA2I],
             w=[B_("a2a_out%d" % s)])

    if os.environ.get("MK_DBG"):
        dbg_xis = nc.dram_tensor("dbg_xis", [128, CH * SWH], BF16,
                                 kind="ExternalOutput").ap()
        dma("gpsimd", dbg_xis, XIs[:], r=[bXI], w=[B_("outdma")])
        dbg_hms = nc.dram_tensor("dbg_hms", [128, CH * SWH], BF16,
                                 kind="ExternalOutput").ap()
        dma("gpsimd", dbg_hms, HMs[:], r=[bHM], w=[B_("outdma")])
        dbg_hs = nc.dram_tensor("dbg_hs", [128, CH * SWH], F32,
                                kind="ExternalOutput").ap()
        dma("gpsimd", dbg_hs, Hs[:], r=[bH], w=[B_("outdma")])
        dbg_xts = nc.dram_tensor("dbg_xts", [128, CH * SWH], F32,
                                 kind="ExternalOutput").ap()
        dma("gpsimd", dbg_xts, XTs[:], r=[bXT], w=[B_("outdma")])

    E("vector", lambda e: e.memzero(state[:]), w=[B_("state")])
    bDC = [B_("dcroll0"), B_("dcroll1")]
    bUC = [B_("ucroll0"), B_("ucroll1")]
    bBCR = [B_("bcroll0"), B_("bcroll1")]

    def gen_p3(k):
        bAO = B_("a2a_out%d" % k)
        bYC = B_("ycroll")
        for sc in range(NSC):
            pr.phase = "P3_k%d" % k
            i2 = sc % 2
            dma("sync", dcroll[i2][:], a2a_out[k][sc, 0:128, :],
                r=[bAO], w=[bDC[i2]])
            dma("sync", ucroll[i2][:], a2a_out[k][sc, 128:256, :],
                r=[bAO], w=[bUC[i2]])
            dma("sync", bcroll[i2][:], a2a_out[k][sc, 256:288, :],
                r=[bAO], w=[bBCR[i2]])
            E("tensor", lambda e, i2=i2: e.matmul(
                p_repA[:, 0:SC], idBb[:], bcroll[i2][:],
                start=True, stop=True),
              r=[B_("idBb"), bBCR[i2]], w=[bRepA])
            E("scalar", lambda e: e.copy(BTt[:], p_repA[:, 0:SC]),
              r=[bRepA], w=[B_("BTt")])
            E("tensor", lambda e, i2=i2: e.matmul(
                p_repA[:, 0:SC], idCb[:], bcroll[i2][:],
                start=True, stop=True),
              r=[B_("idCb"), bBCR[i2]], w=[bRepA])
            E("scalar", lambda e: e.copy(CTt[:], p_repA[:, 0:SC]),
              r=[bRepA], w=[B_("CTt")])
            for ss in range(NSL):
                E("tensor", lambda e, i2=i2, s_=ss: e.matmul(
                    p_repB[:, 0:SC], idrep[:, s_ * 128:(s_ + 1) * 128],
                    dcroll[i2][:], start=True, stop=True),
                  r=[B_("idrep"), bDC[i2]], w=[bRepB])
                E("tensor", lambda e, i2=i2, s_=ss: e.matmul(
                    p_repA[:, 0:SC], idrep[:, s_ * 128:(s_ + 1) * 128],
                    ucroll[i2][:], start=True, stop=True),
                  r=[B_("idrep"), bUC[i2]], w=[bRepA])
                E("scalar", lambda e, s_=ss: e.activation(
                    out=DAt[:], in_=p_repB[:, 0:SC], func=AF.Exp,
                    scale=Amat2[:, s_:s_ + 1]),
                  r=[bRepB, B_("Amat2")], w=[B_("DAt")])
                E("vector", lambda e: e.tensor_tensor(
                    out=BXt[:], in0=p_repA[:, 0:SC], in1=BTt[:], op=OP.mult),
                  r=[bRepA, B_("BTt")], w=[B_("BXt")])
                E("vector", lambda e, s_=ss: e.tensor_tensor_scan(
                    out=HSt[:], data0=DAt[:], data1=BXt[:],
                    initial=state[:, s_:s_ + 1], op0=OP.mult, op1=OP.add),
                  r=[B_("DAt"), B_("BXt"), B_("state")], w=[B_("HSt")])
                E("scalar", lambda e, s_=ss: e.copy(
                    state[:, s_:s_ + 1], HSt[:, SC - 1:SC]),
                  r=[B_("HSt")], w=[B_("state")])
                E("vector", lambda e: e.tensor_mul(HCt[:], HSt[:], CTt[:]),
                  r=[B_("HSt"), B_("CTt")], w=[B_("HCt")])
                E("tensor", lambda e, s_=ss: e.matmul(
                    p_y[:, 0:SC], redrepb[:, s_ * 128:(s_ + 1) * 128], HCt[:],
                    start=(s_ == 0), stop=(s_ == NSL - 1),
                    skip_group_check=True),
                  r=[B_("redrepb"), B_("HCt")], w=[bYps])
            E("scalar", lambda e: e.copy(ycroll[:], p_y[:, 0:SC]),
              r=[bYps], w=[bYC])
            dma("scalar", ya_in[k][sc, :, :], ycroll[:], r=[bYC],
                w=[B_("ya_in%d" % k)])
            yield
        pr.phase = "P3_k%d" % k
        coll("AllToAll", ya_in[k], ya_out[k], r=[B_("ya_in%d" % k)],
             w=[B_("ya_out%d" % k)])

    def gen_p4(k):
        ph = "P4_k%d" % k
        pr.phase = ph
        bYAO = B_("ya_out%d" % k)
        for kc in range(CH):
            dma("sync", YTK[:], ya_out[k][kc, :, :], r=[bYAO], w=[B_("YTK")])
            dma("sync", XCr[:], xc_h[k][:, kc * SW:(kc + 1) * SW],
                r=[B_("xc_h%d" % k)], w=[B_("XCr")])
            dma("sync", ZSr[:], zs_h[k][:, kc * SW:(kc + 1) * SW],
                r=[B_("zs_h%d" % k)], w=[B_("ZSr")])
            E("vector", lambda e, kk=kc: e.scalar_tensor_tensor(
                out=Y2s[:, kk * SW:(kk + 1) * SW], in0=XCr[:],
                scalar=vcs[:, VOFF["Dp"] + kk:VOFF["Dp"] + kk + 1], in1=YTK[:],
                op0=OP.mult, op1=OP.add),
              r=[B_("XCr"), B_("YTK"), bVcs], w=[bHM])
            E("vector", lambda e, kk=kc: e.tensor_tensor(
                out=Y2s[:, kk * SW:(kk + 1) * SW], in0=ZSr[:],
                in1=Y2s[:, kk * SW:(kk + 1) * SW], op=OP.mult),
              r=[B_("ZSr"), bHM], w=[bHM])
        yield
        pr.phase = ph
        bWoutR = [B_("woutroll0"), B_("woutroll0")]
        for mc in range(CH):
            wt, wb = woutroll[mc % 2], bWoutR[mc % 2]
            dma("sync", wt[:].rearrange("p (k c) -> p k c", k=CH),
                wb_out.rearrange("(k p) c -> k p c", p=128)
                .transpose([1, 0, 2])[:, :, mc * 128:(mc + 1) * 128],
                r=[bWhbm], w=[wb])
            pb, pt = bMain[mc % 2], p_main[mc % 2]
            E("tensor", mk_mm_kchain(pt[:, 0:SW], wt, 0, 128,
                                     Y2s, 0, SW, CH, SW),
              r=[wb, bHM], w=[pb])
            dma("sync", HSr[:], hsum_h[k][:, mc * SW:(mc + 1) * SW],
                r=[B_("hsum_h%d" % k)], w=[B_("HSr")])
            E("vector", lambda e, mm=mc, pt=pt: e.tensor_tensor(
                out=X2s[:, mm * SW:(mm + 1) * SW], in0=pt[:, 0:SW],
                in1=HSr[:], op=OP.add),
              r=[pb, B_("HSr")], w=[bXT])
        yield
        pr.phase = ph
        rep4 = (p_main[0], bMain[0], p_main[1], bMain[1])
        norm_stats(X2s, bXT, SW, 0, SW, True, rep=rep4)
        ln_apply(X2s, bXT, SW, 0, SW, H2s, bPA, SW, 0,
                 (vcs, VOFF["ln2_g"]), (vcs, VOFF["ln2_b"]),
                 extra_rbufs=[bVcs], rep=rep4)
        yield
        pr.phase = ph
        bW1R = [B_("w1roll0"), B_("w1roll0")]
        bW2R = B_("w2roll")
        for half in range(2):
            for mc in range(16):
                gm = half * 16 + mc
                wt, wb = w1roll[mc % 2], bW1R[mc % 2]
                dma("sync", wt[:].rearrange("p (k c) -> p k c", k=CH),
                    wb_ff1.rearrange("(k p) c -> k p c", p=128)
                    .transpose([1, 0, 2])[:, :, gm * 128:(gm + 1) * 128],
                    r=[bWhbm], w=[wb])
                pb, pt = bMain[mc % 2], p_main[mc % 2]
                E("tensor", mk_mm_kchain(pt[:, 0:SW], wt, 0, 128,
                                         H2s, 0, SW, CH, SW),
                  r=[wb, bPA], w=[pb])
                E("scalar", lambda e, mm=mc, gm=gm, pt=pt: e.activation(
                    out=FMs[:, mm * SW:(mm + 1) * SW], in_=pt[:, 0:SW],
                    func=AF.Relu, bias=fb1s[:, gm:gm + 1], scale=1.0),
                  r=[pb, B_("fb1s")], w=[bH1])
            for mc in range(CH):
                dma("sync", w2roll[:].rearrange("p (k c) -> p k c", k=16),
                    wb_ff2.rearrange("(k p) c -> k p c", p=128)
                    .transpose([1, 0, 2])[:, half * 16:(half + 1) * 16,
                                          mc * 128:(mc + 1) * 128],
                    r=[bWhbm], w=[bW2R])
                pb, pt = bMain[mc % 2], p_main[mc % 2]
                E("tensor", mk_mm_kchain(pt[:, 0:SW], w2roll, 0, 128,
                                         FMs, 0, SW, 16, SW),
                  r=[bW2R, bH1], w=[pb])
                if half == 0:
                    E("vector", lambda e, mm=mc, pt=pt: e.scalar_tensor_tensor(
                        out=X2s[:, mm * SW:(mm + 1) * SW], in0=pt[:, 0:SW],
                        scalar=vcs[:, VOFF["fb2"] + mm:VOFF["fb2"] + mm + 1],
                        in1=X2s[:, mm * SW:(mm + 1) * SW],
                        op0=OP.add, op1=OP.add),
                      r=[pb, bVcs, bXT], w=[bXT])
                else:
                    E("vector", lambda e, mm=mc, pt=pt: e.tensor_tensor(
                        out=X2s[:, mm * SW:(mm + 1) * SW], in0=pt[:, 0:SW],
                        in1=X2s[:, mm * SW:(mm + 1) * SW], op=OP.add),
                      r=[pb, bXT], w=[bXT])
            yield
            pr.phase = ph
        yield
        pr.phase = ph
        for kc in range(CH):
            dma("sync", XINr[:],
                xT.rearrange("(kk p) c -> kk p c", p=128)
                [kc, :, k * SWH + HALO:(k + 1) * SWH],
                w=[B_("XINr")])
            E("vector", lambda e, kk=kc: e.scalar_tensor_tensor(
                out=XMs[:, kk * SW:(kk + 1) * SW],
                in0=X2s[:, kk * SW:(kk + 1) * SW],
                scalar=modf[:, 2 * CH + kk:2 * CH + kk + 1], in1=XINr[:],
                op0=OP.mult, op1=OP.add),
              r=[bXT, bModf, B_("XINr")], w=[bH])
        yield
        pr.phase = ph
        norm_stats(XMs, bH, SW, 0, SW, True, rep=rep4)
        ln_apply(XMs, bH, SW, 0, SW, H3s, bPB, SW, 0,
                 (modf, 4 * CH), (modf, 3 * CH), extra_rbufs=[bModf],
                 rep=rep4)
        yield
        pr.phase = ph
        for ts_ in range(4):
            tr_seq([(H3s[:, kc * SW + ts_ * 128:kc * SW + ts_ * 128 + 128],
                     bPB, kc,
                     H3T[:, ts_ * 1024 + kc * 128:ts_ * 1024 + (kc + 1) * 128],
                     bXI) for kc in range(CH)])
        for ts_ in range(4):
            emit_quant(H3T, ts_ * 1024, 1024, bXI, H3T, ts_ * 1024, bXI)
            tr_seq([(H3T[:, ts_ * 1024 + kc * 128:ts_ * 1024 + (kc + 1) * 128],
                     bXI, kc,
                     H3Q[:, kc * SW + ts_ * 128:kc * SW + ts_ * 128 + 128],
                     bXC) for kc in range(CH)])
        yield
        pr.phase = ph
        bWGU = [B_("wguroll0"), B_("wguroll1")]
        wgu3 = wgu_lin.rearrange("(kk p) c -> kk p c", p=128).transpose([1, 0, 2])
        for ts_ in range(4):
            for it in range(6):
                iw = 512 if it < 5 else 256
                wtg, wbg = wguroll[it % 2], bWGU[it % 2]
                wtu, wbu = wguroll[(it + 1) % 2], bWGU[(it + 1) % 2]
                dma("sync", wtg[:, 0:CH * iw].rearrange("p (k c) -> p k c", k=CH),
                    wgu3[:, :, it * 512:it * 512 + iw], r=[bGU], w=[wbg])
                dma("sync", wtu[:, 0:CH * iw].rearrange("p (k c) -> p k c", k=CH),
                    wgu3[:, :, DI + it * 512:DI + it * 512 + iw],
                    r=[bGU], w=[wbu])

                def mk_gu(wref, t_, ww, pdst):
                    def fn(e, wref=wref, t_=t_, ww=ww, pdst=pdst):
                        last = None
                        for kk in range(CH):
                            last = e.matmul(
                                pdst[:, 0:ww],
                                H3Q[:, kk * SW + t_ * 128:
                                    kk * SW + t_ * 128 + 128],
                                wref[:, kk * ww:(kk + 1) * ww],
                                start=(kk == 0), stop=(kk == CH - 1))
                        return last
                    return fn
                E("tensor", mk_gu(wtg, ts_, iw, p_mainA), r=[wbg, bXC],
                  w=[bMain[0]])
                E("tensor", mk_gu(wtu, ts_, iw, p_mainB), r=[wbu, bXC],
                  w=[bMain[1]])
                E("scalar", lambda e, ww=iw: e.activation(
                    out=GSs[:, 0:ww], in_=p_mainA[:, 0:ww], func=AF.Silu),
                  r=[bMain[0]], w=[B_("GSs")])
                E("vector", lambda e, i_=it, ww=iw: e.tensor_tensor(
                    out=HMT[:, i_ * 512:i_ * 512 + ww], in0=GSs[:, 0:ww],
                    in1=p_mainB[:, 0:ww], op=OP.mult),
                  r=[B_("GSs"), bMain[1]], w=[bPB])
            emit_quant(HMT, 0, DI, bPB, HMT, 0, bPB)
            for g0 in range(0, 22, 8):
                tr_seq([(HMT[:, ib * 128:(ib + 1) * 128], bPB, ib % 8,
                         HMQT[:, ib * SW + ts_ * 128:ib * SW + ts_ * 128 + 128],
                         bPA) for ib in range(g0, min(g0 + 8, 22))])
            yield
            pr.phase = ph
        yield
        pr.phase = ph
        bWDN = [B_("wdnroll0"), B_("wdnroll0")]
        wdn3 = wdn_lin.rearrange("(kk p) c -> kk p c", p=128).transpose([1, 0, 2])
        for mc in range(CH):
            wtd, wbd = wdnroll[mc % 2], bWDN[mc % 2]
            dma("sync", wtd[:].rearrange("p (k c) -> p k c", k=22),
                wdn3[:, :, mc * 128:(mc + 1) * 128], r=[bDN], w=[wbd])
            pb, pt = bMain[mc % 2], p_main[mc % 2]
            E("tensor", mk_mm_kchain(pt[:, 0:SW], wtd, 0, 128,
                                     HMQT, 0, SW, 22, SW),
              r=[wbd, bPA], w=[pb])
            E("vector", lambda e, mm=mc, pt=pt: e.scalar_tensor_tensor(
                out=OUTr[:], in0=pt[:, 0:SW],
                scalar=modf[:, 5 * CH + mm:5 * CH + mm + 1],
                in1=XMs[:, mm * SW:(mm + 1) * SW], op0=OP.mult, op1=OP.add),
              r=[pb, bModf, bH], w=[B_("OUTr")])
            dma("sync",
                out_fm.rearrange("(kk p) c -> kk p c", p=128)
                [mc, :, k * SW:(k + 1) * SW],
                OUTr[:], r=[B_("OUTr")], w=[B_("outdma")])

    for _ in gen_p3(0):
        pass
    for k in range(1, NST):
        ga, gb = gen_p4(k - 1), gen_p3(k)
        da = db = False
        while not (da and db):
            if not da:
                try:
                    next(ga)
                except StopIteration:
                    da = True
            if not db:
                try:
                    next(gb)
                except StopIteration:
                    db = True
    for _ in gen_p4(NST - 1):
        pass
    if os.environ.get("MK_DBG"):
        dbg_state = nc.dram_tensor("dbg_state", [128, NS], F32,
                                   kind="ExternalOutput").ap()
        dma("gpsimd", dbg_state, state[:], r=[B_("state")], w=[B_("outdma")])
        dbg_bt = nc.dram_tensor("dbg_bt", [128, SC], BF16,
                                kind="ExternalOutput").ap()
        dma("gpsimd", dbg_bt, BTt[:], r=[B_("BTt")], w=[B_("outdma")])
        dbg_ct = nc.dram_tensor("dbg_ct", [128, SC], BF16,
                                kind="ExternalOutput").ap()
        dma("gpsimd", dbg_ct, CTt[:], r=[B_("CTt")], w=[B_("outdma")])
        dbg_bc = nc.dram_tensor("dbg_bc", [32, SC], BF16,
                                kind="ExternalOutput").ap()
        dma("gpsimd", dbg_bc, bcroll[1][:], r=[bBCR[1]], w=[B_("outdma")])
        dbg_da = nc.dram_tensor("dbg_da", [128, SC], BF16,
                                kind="ExternalOutput").ap()
        dma("gpsimd", dbg_da, DAt[:], r=[B_("DAt")], w=[B_("outdma")])
        dbg_bx = nc.dram_tensor("dbg_bx", [128, SC], BF16,
                                kind="ExternalOutput").ap()
        dma("gpsimd", dbg_bx, BXt[:], r=[B_("BXt")], w=[B_("outdma")])
        def dbg_copy(name, src_ap, shape, dtype, rbuf):
            o = nc.dram_tensor("dbg_" + name, list(shape), dtype,
                               kind="ExternalOutput").ap()
            dma("gpsimd", o, src_ap, r=[rbuf], w=[B_("outdma")])
        dbg_copy("modf", modf[:], [128, 6 * CH], F32, bModf)
        dbg_copy("amat", Amat[:], [128, NS], F32, B_("Amat"))
        dbg_copy("hsum", hsum_h, [NST, 128, CH * SW], F32, B_("hsum_h0"))
        dbg_copy("xc", xc_h, [NST, 128, CH * SW], BF16, B_("xc_h0"))
        dbg_copy("zs", zs_h, [NST, 128, CH * SW], BF16, B_("zs_h0"))
        dbg_copy("a2a_in0", a2a_in[0], [NCORES, 288, SW], BF16, B_("a2a_in0"))
        dbg_copy("a2a_out0", a2a_out[0], [NCORES, 288, SW], BF16, B_("a2a_out0"))
        dbg_copy("ya_in0", ya_in[0], [NCORES, 128, SW], BF16, B_("ya_in0"))
        dbg_copy("ya_out0", ya_out[0], [NCORES, 128, SW], BF16, B_("ya_out0"))
        dbg_copy("wgu_lin", wgu_lin, [D, 2 * DI], BF16, bGU)
        dbg_copy("wdn_lin", wdn_lin, [22 * 128, D], BF16, bDN)
        dbg_copy("modout", mod_out, [NCORES, 768], F32, B_("mod_out"))
    E("sync", lambda e: e.nop(), r=[B_("outdma")])

    pr.finalize()
    if skip_block:
        return nc, pr
    with contextlib.ExitStack() as stack:
        sems = {}
        for key in pr.semkeys():
            sems[key] = stack.enter_context(nc.semaphore(key))
        block = stack.enter_context(nc.Block())
        block.sync(lambda e: pr.run_engine("sync", e, sems))
        block.scalar(lambda e: pr.run_engine("scalar", e, sems))
        block.tensor(lambda e: pr.run_engine("tensor", e, sems))
        block.vector(lambda e: pr.run_engine("vector", e, sems))
        block.gpsimd(lambda e: pr.run_engine("gpsimd", e, sems))
    return nc, pr


_NC_CACHE = None
_LAST_RES = None


def _get_nc():
    global _NC_CACHE
    if _NC_CACHE is None:
        _NC_CACHE = build_nc()
    return _NC_CACHE


def kernel(**inputs):
    x = np.asarray(inputs["L0_attn_x"])
    xf = np.ascontiguousarray(x[0])

    def vec8(v):
        return np.ascontiguousarray(np.asarray(v, np.float32).reshape(CH, 128).T)

    vec_names = [("ln1_g", "L0_attn_ln1_g"), ("ln1_b", "L0_attn_ln1_b"),
                 ("ln2_g", "L0_attn_ln2_g"), ("ln2_b", "L0_attn_ln2_b"),
                 ("rms_w", "L0_attn_rms_w"), ("conv_b", "L0_attn_conv_b"),
                 ("dtb", "L0_attn_dt_proj_b"), ("Dp", "L0_attn_D_param"),
                 ("fb2", "L0_attn_ffn_b2"), ("cvec", "L0_c")]
    vecs = np.zeros((128, CH * NV), np.float32)
    for name, key in vec_names:
        vecs[:, VOFF[name]:VOFF[name] + CH] = vec8(np.asarray(inputs[key]).reshape(-1))
    fb1 = np.ascontiguousarray(
        np.asarray(inputs["L0_attn_ffn_b1"], np.float32)
        .reshape(DFF // 128, 128).T)
    convw = np.ascontiguousarray(
        np.asarray(inputs["L0_attn_conv_w"], np.float32)
        .reshape(CH, 128, 4).transpose(1, 0, 2).reshape(128, CH * 4))

    ident = np.eye(128, dtype=np.float32)
    ones1_v = np.ones((1, 128), np.float32)
    onesum_v = np.ones((128, 1), np.float32)
    idrep = np.zeros((128, NSL * 128), np.float32)
    idB = np.zeros((2 * NS, 128), np.float32)
    idC = np.zeros((2 * NS, 128), np.float32)
    redrep = np.zeros((128, NSL * 128), np.float32)
    for p in range(128):
        idB[p % 16, p] = 1.0
        idC[16 + p % 16, p] = 1.0
    for ssl in range(NSL):
        for p in range(128):
            idrep[8 * ssl + p // 16, ssl * 128 + p] = 1.0
        for kk in range(128):
            redrep[kk, ssl * 128 + 8 * ssl + kk // 16] = 1.0

    A_log = np.asarray(inputs["L0_attn_A_log"], np.float32)
    w_gate = np.asarray(inputs["L0_ffn_w_gate"], np.float32)
    w_up = np.asarray(inputs["L0_ffn_w_up"], np.float32)
    w_down = np.asarray(inputs["L0_ffn_w_down"], np.float32)
    wmod = np.asarray(inputs["L0_wmod"], np.float32)
    gu_tiles = np.concatenate([w_gate, w_up], 0).reshape(44, 128, 1024)

    shared = dict(
        w_inproj=np.ascontiguousarray(np.asarray(inputs["L0_attn_in_proj_w"], np.float32)),
        w_xproj=np.ascontiguousarray(np.asarray(inputs["L0_attn_x_proj_w"], np.float32)),
        w_dt=np.ascontiguousarray(np.asarray(inputs["L0_attn_dt_proj_w"], np.float32)),
        w_out=np.ascontiguousarray(np.asarray(inputs["L0_attn_out_proj_w"], np.float32)),
        w_ff1=np.ascontiguousarray(np.asarray(inputs["L0_attn_ffn_w1"], np.float32)),
        w_ff2=np.ascontiguousarray(np.asarray(inputs["L0_attn_ffn_w2"], np.float32)),
        vecs=vecs, fb1=fb1, convw=convw, ident=ident, ones1_in=ones1_v,
        onesum_in=onesum_v, idrep=idrep, idB=idB, idC=idC, redrep=redrep,
        fconst=np.tile(np.array([[EPS, 1.0, 1e-12, -1e-12]], np.float32),
                       (128, 1)),
        fconst2=np.tile(np.array([[-1.0, -0.25, 0.25]], np.float32), (128, 1)),
    )

    in_maps = []
    for c in range(NCORES):
        xTc = np.zeros((D, NST * SWH), np.float32)
        for s in range(NST):
            g = 8 * s + c
            t0 = SW * g
            lo = max(t0 - HALO, 0)
            seg = xf[lo:t0 + SW].T
            xTc[:, s * SWH + (SWH - seg.shape[1]):(s + 1) * SWH] = seg
        A_sl = A_log[128 * c:128 * (c + 1)]
        A_rearr = np.zeros((128, NS), np.float32)
        for ssl in range(NSL):
            A_rearr[:, ssl] = A_sl[8 * ssl:8 * ssl + 8, :].reshape(-1)
        my_gu = np.zeros((6, 128, 1024), np.float32)
        for t in range(44):
            if t // 6 == c:
                my_gu[t % 6] = gu_tiles[t]
        my_dn = np.zeros((8, 128, 384), np.float32)
        my_dn[:, :, 0:352] = w_down[:, 352 * c:352 * (c + 1)].reshape(8, 128, 352)
        m = dict(shared)
        m.update(xT=xTc, A_re=A_rearr, wq_gu=my_gu, wq_dn=my_dn,
                 wmod_sl=np.ascontiguousarray(wmod[:, 768 * c:768 * (c + 1)]))
        in_maps.append(m)

    nc = _get_nc()
    res = run_bass_kernel_spmd(nc, in_maps, core_ids=list(range(NCORES)))
    global _LAST_RES
    _LAST_RES = res.results

    out = np.empty((SEQ, D), np.float32)
    for c in range(NCORES):
        ofm = res.results[c]["out_fm"]
        for s in range(NST):
            g = 8 * s + c
            out[SW * g:SW * (g + 1), :] = ofm[:, s * SW:(s + 1) * SW].T
    return out[None, :, :]
